# version 26
# baseline (speedup 1.0000x reference)
import numpy as np
import concourse.bass as bass
import concourse.mybir as mybir
from concourse.bass_utils import run_bass_kernel_spmd

F32 = mybir.dt.float32
ALU = mybir.AluOpType
AX = mybir.AxisListType
ACT = mybir.ActivationFunctionType

P = 128
D = 2048
NH = 16
KC = D // P
TTH = 4
HALVES = 2
TOK_CORE = 1024
EPS = 1e-5

XA = 0
XB = 8192
STG = 8192
SCRA = 16384
SCRB = 18432
JNK = 20480
FMOFF = 22528
WVF = 30720
WOF = 32768
BOT = 34816
WDO = 36864
WUO = 38912
BUO = 40960
VSF = 43008
GOFF = 43520
WFM = 44032
HOFF = 44544
HWOFF = 45056
BIAS = 45568
BVF = BIAS + 4104
IDENT = BVF + 128
ONES = IDENT + 128
EMAT = ONES + 512
WRC = EMAT + 128
BDC = WRC + 128
STAT = BDC + 1
WACC_OFF_REL = 48
ARENA_COLS = STAT + 48 + 2048


def build_nc(stage="full"):
    nc = bass.Bass("TRN2", target_bir_lowering=False,
                   detect_race_conditions=False)

    x_h = nc.dram_tensor("x", [TOK_CORE, D], F32, kind="ExternalInput").ap()
    wqv_h = nc.dram_tensor("wqkv_v", [D, D], F32, kind="ExternalInput").ap()
    bqv_h = nc.dram_tensor("bqkv_v", [1, D], F32, kind="ExternalInput").ap()
    wout_h = nc.dram_tensor("wout", [D, D], F32, kind="ExternalInput").ap()
    bout_h = nc.dram_tensor("bout", [1, D], F32, kind="ExternalInput").ap()
    wr_h = nc.dram_tensor("wr", [D, 8], F32, kind="ExternalInput").ap()
    br_h = nc.dram_tensor("br", [1, 8], F32, kind="ExternalInput").ap()
    wd_h = nc.dram_tensor("wd", [D, P], F32, kind="ExternalInput").ap()
    bd_h = nc.dram_tensor("bd", [P, 1], F32, kind="ExternalInput").ap()
    wu_h = nc.dram_tensor("wu", [P, D], F32, kind="ExternalInput").ap()
    bu_h = nc.dram_tensor("bu", [8, D], F32, kind="ExternalInput").ap()
    id_h = nc.dram_tensor("ident", [P, P], F32, kind="ExternalInput").ap()
    on_h = nc.dram_tensor("onesrow", [1, 512], F32, kind="ExternalInput").ap()
    em_h = nc.dram_tensor("emat", [8, P], F32, kind="ExternalInput").ap()
    y_h = nc.dram_tensor("y", [TOK_CORE, D], F32, kind="ExternalOutput").ap()
    dump_h = None
    if stage != "full":
        dump_h = nc.dram_tensor("dump", [P, 24576], F32,
                                kind="ExternalOutput").ap()

    arena_g = nc.alloc_sbuf_tensor("arena", [P, ARENA_COLS], F32)
    A = arena_g.ap()

    ps_mm = [nc.alloc_psum_tensor(f"ps_mm{i}", [P, 512], F32).ap() for i in range(2)]
    ps_tp = [nc.alloc_psum_tensor(f"ps_tp{i}", [P, 512], F32).ap() for i in range(4)]
    ps_sm = nc.alloc_psum_tensor("ps_sm", [P, 512], F32).ap()
    ps_we = nc.alloc_psum_tensor("ps_we", [P, 512], F32).ap()

    sem_names = ["s_cst", "s_x", "s_stg", "s_stg2", "s_wvr", "s_z", "s_wv", "s_bvf",
                 "s_wm", "s_bot", "s_wfo", "s_wfov", "s_wfog", "s_wvv", "s_sq", "s_var", "s_std", "s_xn", "s_tpx",
                 "s_fme", "s_mmq", "s_evq", "s_mmo", "s_evo", "s_bo", "s_bta",
                 "s_rps", "s_lg", "s_gt", "s_tpw", "s_wfm", "s_dps",
                 "s_hfm", "s_eps", "s_hw", "s_ups", "s_x2", "s_out",
                 "self_v", "self_s", "self_t"]

    def xt(h, tt):
        base = XA if h == 0 else XB
        return A[:, base + tt * D: base + (tt + 1) * D]

    fm = A[:, FMOFF:FMOFF + 8192]

    def fm_st(tt, k):
        off = FMOFF + k * 512 + tt * P
        return A[:, off:off + P]

    def scr(tt):
        return A[:, (SCRA if tt % 2 == 0 else SCRB):][:, 0:D]

    stg = A[:, STG:STG + 8192]
    jnk = A[:, JNK:JNK + D]
    bias_row = A[0:1, BIAS:BIAS + 4104]
    bvf_row = A[0:1, BVF:BVF + P]
    ident_s = A[:, IDENT:IDENT + P]
    ones_s = A[0:1, ONES:ONES + 512]
    emat_s = A[0:8, EMAT:EMAT + P]
    wr_s = A[:, WRC:WRC + P]
    bd_s = A[:, BDC:BDC + 1]
    st_sum = A[:, STAT + 0:STAT + 4]
    st_sq = A[:, STAT + 4:STAT + 8]
    st_mean = A[:, STAT + 8:STAT + 12]
    st_ms = A[:, STAT + 12:STAT + 16]
    st_var = A[:, STAT + 16:STAT + 20]
    st_sqv = A[:, STAT + 20:STAT + 24]
    st_rstd = A[:, STAT + 24:STAT + 28]
    st_g = [A[:, STAT + 28 + i:STAT + 29 + i] for i in range(4)]
    wvf = A[:, WVF:WVF + 2048]
    wof = A[:, WOF:WOF + 2048]
    botT = A[:, BOT:BOT + 2048]
    wd_s = A[:, WDO:WDO + 2048]
    wu_s = A[:, WUO:WUO + 2048]
    bu_s = A[0:8, BUO:BUO + 2048]
    vsf = A[:, VSF:VSF + 512]
    w_fm = A[0:8, WFM:WFM + 512]
    h_fm = A[:, HOFF:HOFF + 512]
    hw_fm = A[:, HWOFF:HWOFF + 512]

    def gate_tile(tt, slot):
        off = GOFF + slot * 32 + tt * 8
        return A[:, off:off + 8]

    g_er = A[:, GOFF:GOFF + 32]
    g_msk = A[:, GOFF + 32:GOFF + 64]
    g_wg = A[:, GOFF + 64:GOFF + 96]
    wacc2 = A[:, STAT + 48:STAT + 48 + 2048]
    gm1 = A[:, STAT + 28:STAT + 32]
    gm2 = A[:, STAT + 32:STAT + 36]
    gdn = A[:, STAT + 36:STAT + 40]
    grd = A[:, STAT + 40:STAT + 44]

    with nc.Block() as blk:
        sems = {}
        from contextlib import ExitStack
        sem_ctx = ExitStack()
        for s in sem_names:
            sems[s] = sem_ctx.enter_context(nc.semaphore(s))

        def program(E, eng):
            cnt = {s: 0 for s in sem_names}

            def inc(s, n=1):
                cnt[s] += n

            def wait(target_eng, s, val=None):
                if eng == target_eng:
                    v = cnt[s] if val is None else val
                    if v > 0:
                        E.wait_ge(sems[s], v)

            def dma(target_eng, out, in_, s, **kw):
                if eng == target_eng:
                    E.dma_start(out=out, in_=in_, **kw).then_inc(sems[s], 16)
                inc(s, 16)

            SELF = {"vector": "self_v", "scalar": "self_s",
                    "tensor": "self_t"}

            def op(target_eng, s, fn):
                if s is None:
                    s = SELF.get(target_eng)
                if eng == target_eng:
                    ins = fn(E)
                    if s is not None:
                        ins.then_inc(sems[s], 1)
                if s is not None:
                    inc(s, 1)

            def fence(target_eng):
                s = SELF[target_eng]
                if eng == target_eng and cnt[s] > 0:
                    E.wait_ge(sems[s], cnt[s])

            def run(target_eng, fn):
                op(target_eng, None, fn)

            def dump_stop(region, ncols, wait_sems):
                for s in wait_sems:
                    wait("gpsimd", s)
                dma("gpsimd", dump_h[:, 0:ncols], region, "s_out")
                wait("gpsimd", "s_out")


            def ln1_pipe(hh):
                wait("vector", "s_std")
                run("vector", lambda E: E.reciprocal(st_rstd, st_sqv))
                fence("vector")
                for tt in range(TTH):
                    wait("vector", "s_tpx", cnt["s_tpx"] - 4)
                    run("vector", lambda E, hh=hh, tt=tt: E.tensor_scalar_sub(
                        scr(tt), xt(hh, tt), st_mean[:, tt:tt + 1]))
                    fence("vector")
                    op("vector", "s_xn",
                       lambda E, tt=tt: E.tensor_scalar_mul(
                           scr(tt), scr(tt), st_rstd[:, tt:tt + 1]))
                    wait("tensor", "s_xn")
                    wait("tensor", "s_cst")
                    wait("tensor", "s_bo", 4)
                    wait("tensor", "s_fme")
                    for b in range(4):
                        def tr4(E, tt=tt, b=b):
                            ins = None
                            for q in range(4):
                                k = b * 4 + q
                                ins = E.transpose(ps_tp[b][:, q * P:(q + 1) * P],
                                                  scr(tt)[:, k * P:(k + 1) * P],
                                                  ident_s)
                            return ins
                        op("tensor", "s_tpx", tr4)
                    wait("scalar", "s_tpx")
                    for b in range(4):
                        def ev4(E, tt=tt, b=b):
                            dst = fm[:, b * 4 * 512 + tt * P:]
                            dst = bass.AP(dst.tensor, dst.offset,
                                          [list(dst.ap[0]), [512, 4], [1, P]])
                            return E.copy(dst, ps_tp[b][:].rearrange(
                                "p (q t) -> p q t", q=4))
                        op("scalar", "s_fme", ev4)

            def ln1_stats(hh):
                wait("scalar", "s_cst")
                wait("scalar", "s_var")
                wait("vector", "s_var")
                for tt in range(TTH):
                    base = 64 * hh
                    wait("scalar", "s_x", base + 16 * (tt + 1))
                    wait("vector", "s_x", base + 16 * (tt + 1))
                    op("scalar", "s_sq",
                       lambda E, hh=hh, tt=tt: E.activation(
                           jnk, xt(hh, tt), ACT.Square,
                           accum_out=st_sq[:, tt:tt + 1]))
                    run("vector", lambda E, hh=hh, tt=tt: E.reduce_sum(
                        st_sum[:, tt:tt + 1], xt(hh, tt), AX.X))
                wait("vector", "s_sq")
                fence("vector")
                run("vector", lambda E: E.tensor_scalar_mul(st_mean, st_sum, 1.0 / D))
                run("vector", lambda E: E.tensor_scalar_mul(st_ms, st_sq, 1.0 / D))
                fence("vector")
                run("vector", lambda E: E.tensor_mul(st_var, st_mean, st_mean))
                fence("vector")
                op("vector", "s_var",
                   lambda E: E.tensor_sub(st_var, st_ms, st_var))
                wait("scalar", "s_var")
                op("scalar", "s_std",
                   lambda E: E.activation(st_sqv, st_var, ACT.Sqrt,
                                          bias=np.float32(EPS)))

            dma("sync", ident_s, id_h, "s_cst")
            dma("sync", ones_s, on_h, "s_cst")
            dma("sync", emat_s, em_h, "s_cst")
            dma("sync", wr_s.rearrange("p (k e) -> p k e", e=8),
                wr_h.rearrange("(k p) e -> p k e", p=P), "s_cst")
            dma("sync", bd_s, bd_h, "s_cst")
            dma("sync", A[0:1, BIAS:BIAS + D], bqv_h, "s_cst")
            dma("sync", A[0:1, BIAS + D:BIAS + 2 * D], bout_h, "s_cst")
            dma("sync", A[0:1, BIAS + 4096:BIAS + 4104], br_h, "s_cst")
            for tt in range(TTH):
                dma("sync", xt(0, tt), x_h[tt * P:(tt + 1) * P, :], "s_x")

            wait("vector", "s_cst")
            def bvf_fold(E):
                sap = bass.AP(bias_row.tensor, bias_row.offset,
                              [list(bias_row.ap[0]), [NH, P], [1, NH]])
                return E.tensor_reduce(bvf_row, sap, AX.X, ALU.add)
            op("vector", "s_bvf", bvf_fold)
            wait("tensor", "s_cst")
            for nb in range(4):
                wait("tensor", "s_bo", nb)
                op("tensor", "s_bot",
                   lambda E, nb=nb: E.matmul(
                       ps_tp[3][:, 0:512], ones_s[:, 0:P],
                       bias_row[:, D + nb * 512:D + (nb + 1) * 512],
                       start=True, stop=True))
                wait("scalar", "s_bot")
                op("scalar", "s_bo",
                   lambda E, nb=nb: E.copy(botT[:, nb * 512:(nb + 1) * 512],
                                           ps_tp[3][:, 0:512]))

            ln1_stats(0)
            ln1_pipe(0)

            wvf_bufs = [stg[:, 0:D], stg[:, D:2 * D], wd_s]
            wof_bufs = [stg[:, 2 * D:3 * D], stg[:, 3 * D:4 * D],
                        A[:, BUO:BUO + D], wu_s]

            for g in range(NH):
                wait("gpsimd", "s_wfog", g - 3)
                dma("gpsimd", wof_bufs[g % 4], wout_h[g * P:(g + 1) * P, :],
                    "s_stg2")

            for g in range(NH):
                wait("sync", "s_wvr", g - 2)
                dma("sync", wvf_bufs[g % 3], wqv_h[g * P:(g + 1) * P, :],
                    "s_stg")
                wait("vector", "s_stg")
                def fold(E, g=g):
                    bb = wvf_bufs[g % 3]
                    sap = bass.AP(bb.tensor, bb.offset,
                                  [list(bb.ap[0]), [NH, P], [1, NH]])
                    return E.tensor_reduce(
                        wvf[:, g * P:(g + 1) * P], sap, AX.X, ALU.add)
                op("vector", "s_wvr", fold)
            for g in range(NH):
                wait("vector", "s_stg2", 16 * (g + 1))
                fence("vector")
                def wacc(E, g=g):
                    a = wof_bufs[g % 4]
                    dst = wof if g % 2 == 0 else wacc2
                    if g <= 1:
                        return E.tensor_copy(dst, a)
                    return E.tensor_add(dst, dst, a)
                op("vector", "s_wfog", wacc)
            fence("vector")
            op("vector", "s_wvv", lambda E: E.tensor_add(wof, wof, wacc2))

            wait("sync", "s_wvr")
            dma("sync", wd_s.rearrange("p (k c) -> p k c", c=P),
                wd_h.rearrange("(k p) c -> p k c", p=P), "s_wm")
            wait("sync", "s_wfog")
            dma("sync", wu_s, wu_h, "s_wm")
            dma("sync", bu_s, bu_h, "s_wm")

            wait("sync", "s_wvr")
            wait("sync", "s_wfog")
            for tt in range(TTH):
                dma("sync", xt(1, tt), x_h[512 + tt * P:512 + (tt + 1) * P, :],
                    "s_x")

            for h in range(HALVES):
                tok0 = h * 512

                if h == 1:
                    ln1_pipe(1)

                if stage == "ln1" and h == 0:
                    dump_stop(fm, 8192, ["s_fme"])
                    return

                wait("tensor", "s_wvr")
                wait("tensor", "s_bvf")
                wait("tensor", "s_fme")
                wait("tensor", "s_evq", h)
                wait("tensor", "s_evo", 16 * h)
                wait("tensor", "s_lg", 4 * h)
                def mmv(E):
                    ins = None
                    for k in range(KC):
                        ins = E.matmul(ps_sm[:, 0:512], wvf[:, k * P:(k + 1) * P],
                                       fm[:, k * 512:(k + 1) * 512],
                                       start=(k == 0), stop=False)
                    ins = E.matmul(ps_sm[:, 0:512],
                                   bvf_row, ones_s[:, 0:512],
                                   start=False, stop=True)
                    return ins
                op("tensor", "s_mmq", mmv)
                wait("scalar", "s_mmq")
                op("scalar", "s_evq", lambda E: E.copy(vsf, ps_sm[:, 0:512]))

                if stage == "vsum" and h == 0:
                    dump_stop(A[:, WVF:WVF + 12800], 12800,
                              ["s_evq", "s_wvr", "s_wv", "s_bvf", "s_bo"])
                    return

                ybank = [ps_mm[0], ps_mm[1], ps_sm, ps_we]
                for tt in range(TTH):
                    for nb in range(4):
                        oidx = (h * TTH + tt) * 4 + nb
                        wait("tensor", "s_wvv")
                        wait("tensor", "s_evq")
                        wait("tensor", "s_lg", 4 * h)
                        wait("tensor", "s_hw", h)
                        wait("tensor", "s_evo", oidx - 3)
                        op("tensor", "s_mmo",
                           lambda E, tt=tt, nb=nb, oidx=oidx: E.matmul(
                               ybank[oidx % 4][:, 0:512],
                               vsf[:, tt * P:(tt + 1) * P],
                               wof[:, nb * 512:(nb + 1) * 512],
                               start=True, stop=True))
                        wait("vector", "s_mmo")
                        wait("vector", "s_x")
                        wait("vector", "s_bo", 4)
                        op("vector", "s_evo",
                           lambda E, h=h, tt=tt, nb=nb, oidx=oidx:
                           E.scalar_tensor_tensor(
                               out=xt(h, tt)[:, nb * 512:(nb + 1) * 512],
                               in0=ybank[oidx % 4][:, 0:512], scalar=1.0,
                               in1=xt(h, tt)[:, nb * 512:(nb + 1) * 512],
                               op0=ALU.mult, op1=ALU.add))
                    fence("vector")
                    op("vector", "s_bta",
                       lambda E, h=h, tt=tt: E.tensor_add(xt(h, tt), xt(h, tt),
                                                          botT))

                if stage == "wout" and h == 0:
                    dump_stop(A[:, XA:XA + 8192], 8192, ["s_evo", "s_bta"])
                    return

                wait("scalar", "s_var")
                wait("vector", "s_var")
                for tt in range(TTH):
                    wait("vector", "s_bta", h * TTH + tt + 1)
                    wait("scalar", "s_bta", h * TTH + tt + 1)
                    op("scalar", "s_sq",
                       lambda E, h=h, tt=tt: E.activation(
                           jnk, xt(h, tt), ACT.Square,
                           accum_out=st_sq[:, tt:tt + 1]))
                    run("vector", lambda E, h=h, tt=tt: E.reduce_sum(
                        st_sum[:, tt:tt + 1], xt(h, tt), AX.X))
                wait("vector", "s_sq")
                fence("vector")
                run("vector", lambda E: E.tensor_scalar_mul(st_mean, st_sum, 1.0 / D))
                run("vector", lambda E: E.tensor_scalar_mul(st_ms, st_sq, 1.0 / D))
                fence("vector")
                run("vector", lambda E: E.tensor_mul(st_var, st_mean, st_mean))
                fence("vector")
                op("vector", "s_var",
                   lambda E: E.tensor_sub(st_var, st_ms, st_var))
                wait("scalar", "s_var")
                op("scalar", "s_std",
                   lambda E: E.activation(st_sqv, st_var, ACT.Sqrt,
                                          bias=np.float32(EPS)))
                wait("vector", "s_std")
                run("vector", lambda E: E.reciprocal(st_rstd, st_sqv))
                fence("vector")
                for tt in range(TTH):
                    wait("vector", "s_tpx", cnt["s_tpx"] - 4)
                    run("vector", lambda E, h=h, tt=tt: E.tensor_scalar_sub(
                        scr(tt), xt(h, tt), st_mean[:, tt:tt + 1]))
                    fence("vector")
                    op("vector", "s_xn",
                       lambda E, tt=tt: E.tensor_scalar_mul(
                           scr(tt), scr(tt), st_rstd[:, tt:tt + 1]))
                    wait("tensor", "s_xn")
                    wait("tensor", "s_fme")
                    for b in range(4):
                        def tr4b(E, tt=tt, b=b):
                            ins = None
                            for q in range(4):
                                k = b * 4 + q
                                ins = E.transpose(ps_tp[b][:, q * P:(q + 1) * P],
                                                  scr(tt)[:, k * P:(k + 1) * P],
                                                  ident_s)
                            return ins
                        op("tensor", "s_tpx", tr4b)
                    wait("scalar", "s_tpx")
                    for b in range(4):
                        def ev4b(E, tt=tt, b=b):
                            dst = fm[:, b * 4 * 512 + tt * P:]
                            dst = bass.AP(dst.tensor, dst.offset,
                                          [list(dst.ap[0]), [512, 4], [1, P]])
                            return E.copy(dst, ps_tp[b][:].rearrange(
                                "p (q t) -> p q t", q=4))
                        op("scalar", "s_fme", ev4b)

                for tt in range(TTH):
                    wait("tensor", "s_fme")
                    wait("tensor", "s_evo")
                    wait("tensor", "s_evq")
                    wait("tensor", "s_lg", 4 * h + tt)
                    def mmr(E, tt=tt):
                        ins = None
                        for k in range(KC):
                            ins = E.matmul(ps_sm[:, 0:8], fm_st(tt, k),
                                           wr_s[:, k * 8:(k + 1) * 8],
                                           start=(k == 0), stop=False)
                        ins = E.matmul(ps_sm[:, 0:8], ones_s[:, 0:P],
                                       bias_row[:, 4096:4104],
                                       start=False, stop=True)
                        return ins
                    op("tensor", "s_rps", mmr)
                    wait("scalar", "s_rps")
                    op("scalar", "s_lg",
                       lambda E, tt=tt: E.activation(gate_tile(tt, 0),
                                                     ps_sm[:, 0:8], ACT.Exp))

                wait("vector", "s_lg")
                er3 = g_er.rearrange("p (t e) -> p t e", e=8)
                msk3 = g_msk.rearrange("p (t e) -> p t e", e=8)
                wg3 = g_wg.rearrange("p (t e) -> p t e", e=8)
                def b1(st):
                    return st.unsqueeze(2).broadcast_to([P, 4, 8])
                run("vector", lambda E: E.tensor_reduce(gm1, er3, AX.X, ALU.max))
                fence("vector")
                run("vector", lambda E: E.tensor_tensor(msk3, er3, b1(gm1),
                                                        ALU.is_lt))
                fence("vector")
                run("vector", lambda E: E.tensor_mul(g_msk, g_msk, g_er))
                fence("vector")
                run("vector", lambda E: E.tensor_reduce(gm2, msk3, AX.X, ALU.max))
                fence("vector")
                run("vector", lambda E: E.tensor_tensor(msk3, er3, b1(gm2),
                                                        ALU.is_ge))
                run("vector", lambda E: E.tensor_add(gdn, gm1, gm2))
                fence("vector")
                run("vector", lambda E: E.tensor_mul(g_wg, g_msk, g_er))
                run("vector", lambda E: E.reciprocal(grd, gdn))
                fence("vector")
                op("vector", "s_gt",
                   lambda E: E.tensor_tensor(wg3, wg3, b1(grd), ALU.mult))

                wait("tensor", "s_wm")
                wait("tensor", "s_fme")
                wait("tensor", "s_hfm", h)
                def mmd(E):
                    ins = None
                    for k in range(KC):
                        ins = E.matmul(ps_tp[1][:, 0:512],
                                       wd_s[:, k * P:(k + 1) * P],
                                       fm[:, k * 512:(k + 1) * 512],
                                       start=(k == 0), stop=(k == KC - 1))
                    return ins
                op("tensor", "s_dps", mmd)
                wait("scalar", "s_dps")
                op("scalar", "s_hfm",
                   lambda E: E.activation(h_fm, ps_tp[1][:, 0:512], ACT.Relu,
                                          bias=bd_s))

                wait("tensor", "s_gt")
                wait("tensor", "s_wfm", h)
                def trw(E):
                    ins = None
                    for tt in range(TTH):
                        ins = E.transpose(ps_tp[0][0:8, tt * P:(tt + 1) * P],
                                          gate_tile(tt, 2), ident_s)
                    return ins
                op("tensor", "s_tpw", trw)
                wait("scalar", "s_tpw")
                op("scalar", "s_wfm",
                   lambda E: E.copy(w_fm, ps_tp[0][0:8, 0:512]))

                wait("tensor", "s_wfm")
                wait("tensor", "s_hw", h)
                wait("tensor", "s_evo")
                op("tensor", "s_eps",
                   lambda E: E.matmul(ps_we[:, 0:512], emat_s, w_fm,
                                      start=True, stop=True))
                wait("vector", "s_eps")
                wait("vector", "s_hfm")
                op("vector", "s_hw",
                   lambda E: E.tensor_tensor(hw_fm, h_fm, ps_we[:, 0:512],
                                             ALU.mult))

                if h == 0:
                    ln1_stats(1)

                fence("vector")
                for tt in range(TTH):
                    for nt in range(4):
                        uidx = (h * TTH + tt) * 4 + nt
                        wait("tensor", "s_hw")
                        wait("tensor", "s_evo")
                        wait("tensor", "s_x2", uidx - 1)
                        def mmu(E, tt=tt, nt=nt, uidx=uidx):
                            pb = ps_mm[uidx % 2]
                            E.matmul(pb[:, 0:512],
                                     hw_fm[:, tt * P:(tt + 1) * P],
                                     wu_s[:, nt * 512:(nt + 1) * 512],
                                     start=True, stop=False)
                            return E.matmul(pb[:, 0:512],
                                            w_fm[:, tt * P:(tt + 1) * P],
                                            bu_s[:, nt * 512:(nt + 1) * 512],
                                            start=False, stop=True)
                        op("tensor", "s_ups", mmu)
                        wait("vector", "s_ups")
                        op("vector", "s_x2",
                           lambda E, h=h, tt=tt, nt=nt, uidx=uidx:
                           E.scalar_tensor_tensor(
                               out=xt(h, tt)[:, nt * 512:(nt + 1) * 512],
                               in0=ps_mm[uidx % 2][:, 0:512], scalar=1.0,
                               in1=xt(h, tt)[:, nt * 512:(nt + 1) * 512],
                               op0=ALU.mult, op1=ALU.add))
                    wait("gpsimd", "s_x2")
                    dma("gpsimd", y_h[tok0 + tt * P: tok0 + (tt + 1) * P, :],
                        xt(h, tt), "s_out")

            wait("gpsimd", "s_out")

        @blk.sync
        def _(E):
            program(E, "sync")

        @blk.tensor
        def _(E):
            program(E, "tensor")

        @blk.scalar
        def _(E):
            program(E, "scalar")

        @blk.vector
        def _(E):
            program(E, "vector")

        @blk.gpsimd
        def _(E):
            program(E, "gpsimd")

        sem_ctx.close()
    return nc


def kernel(L0_attn_x, L0_attn_Wqkv, L0_attn_bqkv, L0_attn_Wout, L0_attn_bout,
           L0_ffn_Wr, L0_ffn_br, L0_ffn_Wd, L0_ffn_bd, L0_ffn_Wu, L0_ffn_bu):
    x = np.ascontiguousarray(np.asarray(L0_attn_x, dtype=np.float32).reshape(8192, D))
    wqkv_v = np.ascontiguousarray(
        np.asarray(L0_attn_Wqkv, np.float32).reshape(D, NH, 384)[:, :, 256:384]
        .transpose(0, 2, 1)).reshape(D, D)
    bqkv_v = np.ascontiguousarray(
        np.asarray(L0_attn_bqkv, np.float32).reshape(NH, 384)[:, 256:384]
        .T).reshape(1, D)
    wout = np.ascontiguousarray(np.asarray(L0_attn_Wout, np.float32).reshape(D, D))
    bout = np.ascontiguousarray(np.asarray(L0_attn_bout, np.float32).reshape(1, D))
    wr = np.ascontiguousarray(np.asarray(L0_ffn_Wr, np.float32).reshape(D, 8))
    br = np.ascontiguousarray(np.asarray(L0_ffn_br, np.float32).reshape(1, 8))
    wd = np.ascontiguousarray(
        np.asarray(L0_ffn_Wd, np.float32).transpose(1, 0, 2).reshape(D, P))
    bd = np.ascontiguousarray(np.asarray(L0_ffn_bd, np.float32).reshape(P, 1))
    wu = np.ascontiguousarray(np.asarray(L0_ffn_Wu, np.float32).reshape(P, D))
    bu = np.ascontiguousarray(np.asarray(L0_ffn_bu, np.float32).reshape(8, D))
    ident = np.eye(P, dtype=np.float32)
    onesrow = np.ones((1, 512), dtype=np.float32)
    emat = np.zeros((8, P), dtype=np.float32)
    for e in range(8):
        emat[e, e * 16:(e + 1) * 16] = 1.0

    nc = build_nc()
    in_maps = []
    for c in range(8):
        in_maps.append({
            "x": x[c * TOK_CORE:(c + 1) * TOK_CORE],
            "wqkv_v": wqkv_v, "bqkv_v": bqkv_v, "wout": wout, "bout": bout,
            "wr": wr, "br": br, "wd": wd, "bd": bd, "wu": wu, "bu": bu,
            "ident": ident, "onesrow": onesrow, "emat": emat,
        })
    res = run_bass_kernel_spmd(nc, in_maps, core_ids=list(range(8)))
    out = np.concatenate([np.asarray(r["y"]) for r in res.results], axis=0)
    return out.reshape(1, 8192, D)


# version 29
# speedup vs baseline: 1.0054x; 1.0054x over previous
import numpy as np
import concourse.bass as bass
import concourse.mybir as mybir
from concourse.bass_utils import run_bass_kernel_spmd

F32 = mybir.dt.float32
ALU = mybir.AluOpType
AX = mybir.AxisListType
ACT = mybir.ActivationFunctionType

P = 128
D = 2048
NH = 16
KC = D // P
TTH = 4
HALVES = 2
TOK_CORE = 1024
EPS = 1e-5

XA = 0
XB = 8192
STG = 8192
SCRA = 16384
SCRB = 18432
JNK = 20480
FMOFF = 22528
WVF = 30720
WOF = 32768
BOT = 34816
WDO = 36864
WUO = 38912
BUO = 40960
VSF = 43008
GOFF = 43520
WFM = 44032
HOFF = 44544
HWOFF = 45056
BIAS = 45568
BVF = BIAS + 4104
IDENT = BVF + 128
ONES = IDENT + 128
EMAT = ONES + 512
WRC = EMAT + 128
BDC = WRC + 128
STAT = BDC + 1
WACC_OFF_REL = 48
ARENA_COLS = STAT + 48 + 2048


def build_nc(stage="full"):
    nc = bass.Bass("TRN2", target_bir_lowering=False,
                   detect_race_conditions=False)

    x_h = nc.dram_tensor("x", [TOK_CORE, D], F32, kind="ExternalInput").ap()
    wqv_h = nc.dram_tensor("wqkv_v", [D, D], F32, kind="ExternalInput").ap()
    bqv_h = nc.dram_tensor("bqkv_v", [1, D], F32, kind="ExternalInput").ap()
    wout_h = nc.dram_tensor("wout", [D, D], F32, kind="ExternalInput").ap()
    bout_h = nc.dram_tensor("bout", [1, D], F32, kind="ExternalInput").ap()
    wr_h = nc.dram_tensor("wr", [D, 8], F32, kind="ExternalInput").ap()
    br_h = nc.dram_tensor("br", [1, 8], F32, kind="ExternalInput").ap()
    wd_h = nc.dram_tensor("wd", [D, P], F32, kind="ExternalInput").ap()
    bd_h = nc.dram_tensor("bd", [P, 1], F32, kind="ExternalInput").ap()
    wu_h = nc.dram_tensor("wu", [P, D], F32, kind="ExternalInput").ap()
    bu_h = nc.dram_tensor("bu", [8, D], F32, kind="ExternalInput").ap()
    id_h = nc.dram_tensor("ident", [P, P], F32, kind="ExternalInput").ap()
    on_h = nc.dram_tensor("onesrow", [1, 512], F32, kind="ExternalInput").ap()
    em_h = nc.dram_tensor("emat", [8, P], F32, kind="ExternalInput").ap()
    y_h = nc.dram_tensor("y", [TOK_CORE, D], F32, kind="ExternalOutput").ap()
    dump_h = None
    if stage != "full":
        dump_h = nc.dram_tensor("dump", [P, 24576], F32,
                                kind="ExternalOutput").ap()

    arena_g = nc.alloc_sbuf_tensor("arena", [P, ARENA_COLS], F32)
    A = arena_g.ap()

    ps_mm = [nc.alloc_psum_tensor(f"ps_mm{i}", [P, 512], F32).ap() for i in range(2)]
    ps_tp = [nc.alloc_psum_tensor(f"ps_tp{i}", [P, 512], F32).ap() for i in range(4)]
    ps_sm = nc.alloc_psum_tensor("ps_sm", [P, 512], F32).ap()
    ps_we = nc.alloc_psum_tensor("ps_we", [P, 512], F32).ap()

    sem_names = ["s_cst", "s_x", "s_stg", "s_stg2", "s_wvr", "s_z", "s_wv", "s_bvf",
                 "s_wm", "s_bot", "s_wfo", "s_wfov", "s_wfog", "s_wvv", "s_sq", "s_var", "s_std", "s_xn", "s_tpx",
                 "s_fme", "s_mmq", "s_evq", "s_mmo", "s_evo", "s_bo", "s_bta",
                 "s_rps", "s_lg", "s_gt", "s_tpw", "s_wfm", "s_dps",
                 "s_hfm", "s_eps", "s_hw", "s_ups", "s_x2", "s_out",
                 "self_v", "self_s", "self_t"]

    def xt(h, tt):
        base = XA if h == 0 else XB
        return A[:, base + tt * D: base + (tt + 1) * D]

    fm = A[:, FMOFF:FMOFF + 8192]

    def fm_st(tt, k):
        off = FMOFF + k * 512 + tt * P
        return A[:, off:off + P]

    def scr(tt):
        return A[:, (SCRA if tt % 2 == 0 else SCRB):][:, 0:D]

    stg = A[:, STG:STG + 8192]
    jnk = A[:, JNK:JNK + D]
    bias_row = A[0:1, BIAS:BIAS + 4104]
    bvf_row = A[0:1, BVF:BVF + P]
    ident_s = A[:, IDENT:IDENT + P]
    ones_s = A[0:1, ONES:ONES + 512]
    emat_s = A[0:8, EMAT:EMAT + P]
    wr_s = A[:, WRC:WRC + P]
    bd_s = A[:, BDC:BDC + 1]
    st_sum = A[:, STAT + 0:STAT + 4]
    st_sq = A[:, STAT + 4:STAT + 8]
    st_mean = A[:, STAT + 8:STAT + 12]
    st_ms = A[:, STAT + 12:STAT + 16]
    st_var = A[:, STAT + 16:STAT + 20]
    st_sqv = A[:, STAT + 20:STAT + 24]
    st_rstd = A[:, STAT + 24:STAT + 28]
    st_g = [A[:, STAT + 28 + i:STAT + 29 + i] for i in range(4)]
    wvf = A[:, WVF:WVF + 2048]
    wof = A[:, WOF:WOF + 2048]
    botT = A[:, BOT:BOT + 2048]
    wd_s = A[:, WDO:WDO + 2048]
    wu_s = A[:, WUO:WUO + 2048]
    bu_s = A[0:8, BUO:BUO + 2048]
    vsf = A[:, VSF:VSF + 512]
    w_fm = A[0:8, WFM:WFM + 512]
    h_fm = A[:, HOFF:HOFF + 512]
    hw_fm = A[:, HWOFF:HWOFF + 512]

    def gate_tile(tt, slot):
        off = GOFF + slot * 32 + tt * 8
        return A[:, off:off + 8]

    g_er = A[:, GOFF:GOFF + 32]
    g_msk = A[:, GOFF + 32:GOFF + 64]
    g_wg = A[:, GOFF + 64:GOFF + 96]
    wacc2 = A[:, STAT + 48:STAT + 48 + 2048]
    gm1 = A[:, STAT + 28:STAT + 32]
    gm2 = A[:, STAT + 32:STAT + 36]
    gdn = A[:, STAT + 36:STAT + 40]
    grd = A[:, STAT + 40:STAT + 44]

    with nc.Block() as blk:
        sems = {}
        from contextlib import ExitStack
        sem_ctx = ExitStack()
        for s in sem_names:
            sems[s] = sem_ctx.enter_context(nc.semaphore(s))

        def program(E, eng):
            cnt = {s: 0 for s in sem_names}

            def inc(s, n=1):
                cnt[s] += n

            def wait(target_eng, s, val=None):
                if eng == target_eng:
                    v = cnt[s] if val is None else val
                    if v > 0:
                        E.wait_ge(sems[s], v)

            def dma(target_eng, out, in_, s, **kw):
                if eng == target_eng:
                    E.dma_start(out=out, in_=in_, **kw).then_inc(sems[s], 16)
                inc(s, 16)

            SELF = {"vector": "self_v", "scalar": "self_s",
                    "tensor": "self_t"}

            def op(target_eng, s, fn):
                if s is None:
                    s = SELF.get(target_eng)
                if eng == target_eng:
                    ins = fn(E)
                    if s is not None:
                        ins.then_inc(sems[s], 1)
                if s is not None:
                    inc(s, 1)

            def fence(target_eng):
                s = SELF[target_eng]
                if eng == target_eng and cnt[s] > 0:
                    E.wait_ge(sems[s], cnt[s])

            def run(target_eng, fn):
                op(target_eng, None, fn)

            def dump_stop(region, ncols, wait_sems):
                for s in wait_sems:
                    wait("gpsimd", s)
                dma("gpsimd", dump_h[:, 0:ncols], region, "s_out")
                wait("gpsimd", "s_out")


            def ln1_pipe(hh):
                wait("vector", "s_std")
                run("vector", lambda E: E.reciprocal(st_rstd, st_sqv))
                fence("vector")
                for tt in range(TTH):
                    wait("vector", "s_tpx", cnt["s_tpx"] - 4)
                    run("vector", lambda E, hh=hh, tt=tt: E.tensor_scalar_sub(
                        scr(tt), xt(hh, tt), st_mean[:, tt:tt + 1]))
                    fence("vector")
                    op("vector", "s_xn",
                       lambda E, tt=tt: E.tensor_scalar_mul(
                           scr(tt), scr(tt), st_rstd[:, tt:tt + 1]))
                    wait("tensor", "s_xn")
                    wait("tensor", "s_cst")
                    wait("tensor", "s_bo", 4)
                    wait("tensor", "s_fme")
                    for b in range(4):
                        def tr4(E, tt=tt, b=b):
                            ins = None
                            for q in range(4):
                                k = b * 4 + q
                                ins = E.transpose(ps_tp[b][:, q * P:(q + 1) * P],
                                                  scr(tt)[:, k * P:(k + 1) * P],
                                                  ident_s)
                            return ins
                        op("tensor", "s_tpx", tr4)
                    wait("scalar", "s_tpx")
                    for b in range(4):
                        def ev4(E, tt=tt, b=b):
                            dst = fm[:, b * 4 * 512 + tt * P:]
                            dst = bass.AP(dst.tensor, dst.offset,
                                          [list(dst.ap[0]), [512, 4], [1, P]])
                            return E.copy(dst, ps_tp[b][:].rearrange(
                                "p (q t) -> p q t", q=4))
                        op("scalar", "s_fme", ev4)

            def ln1_stats(hh):
                wait("scalar", "s_cst")
                wait("scalar", "s_var")
                wait("vector", "s_var")
                for tt in range(TTH):
                    base = 64 * hh
                    wait("scalar", "s_x", base + 16 * (tt + 1))
                    wait("vector", "s_x", base + 16 * (tt + 1))
                    op("scalar", "s_sq",
                       lambda E, hh=hh, tt=tt: E.activation(
                           jnk, xt(hh, tt), ACT.Square,
                           accum_out=st_sq[:, tt:tt + 1]))
                    run("vector", lambda E, hh=hh, tt=tt: E.reduce_sum(
                        st_sum[:, tt:tt + 1], xt(hh, tt), AX.X))
                wait("vector", "s_sq")
                fence("vector")
                run("vector", lambda E: E.tensor_scalar_mul(st_mean, st_sum, 1.0 / D))
                run("vector", lambda E: E.tensor_scalar_mul(st_ms, st_sq, 1.0 / D))
                fence("vector")
                run("vector", lambda E: E.tensor_mul(st_var, st_mean, st_mean))
                fence("vector")
                op("vector", "s_var",
                   lambda E: E.tensor_sub(st_var, st_ms, st_var))
                wait("scalar", "s_var")
                op("scalar", "s_std",
                   lambda E: E.activation(st_sqv, st_var, ACT.Sqrt,
                                          bias=np.float32(EPS)))

            dma("sync", ident_s, id_h, "s_cst")
            dma("sync", ones_s, on_h, "s_cst")
            dma("sync", emat_s, em_h, "s_cst")
            dma("sync", wr_s.rearrange("p (k e) -> p k e", e=8),
                wr_h.rearrange("(k p) e -> p k e", p=P), "s_cst")
            dma("sync", bd_s, bd_h, "s_cst")
            dma("sync", A[0:1, BIAS:BIAS + D], bqv_h, "s_cst")
            dma("sync", A[0:1, BIAS + D:BIAS + 2 * D], bout_h, "s_cst")
            dma("sync", A[0:1, BIAS + 4096:BIAS + 4104], br_h, "s_cst")
            for tt in range(TTH):
                dma("sync", xt(0, tt), x_h[tt * P:(tt + 1) * P, :], "s_x")

            wait("vector", "s_cst")
            def bvf_fold(E):
                sap = bass.AP(bias_row.tensor, bias_row.offset,
                              [list(bias_row.ap[0]), [NH, P], [1, NH]])
                return E.tensor_reduce(bvf_row, sap, AX.X, ALU.add)
            op("vector", "s_bvf", bvf_fold)
            wait("tensor", "s_cst")
            for nb in range(4):
                wait("tensor", "s_bo", nb)
                op("tensor", "s_bot",
                   lambda E, nb=nb: E.matmul(
                       ps_tp[3][:, 0:512], ones_s[:, 0:P],
                       bias_row[:, D + nb * 512:D + (nb + 1) * 512],
                       start=True, stop=True))
                wait("scalar", "s_bot")
                op("scalar", "s_bo",
                   lambda E, nb=nb: E.copy(botT[:, nb * 512:(nb + 1) * 512],
                                           ps_tp[3][:, 0:512]))

            ln1_stats(0)
            ln1_pipe(0)

            wvf_bufs = [stg[:, 0:D], stg[:, D:2 * D], wd_s]
            wof_bufs = [stg[:, 2 * D:3 * D], stg[:, 3 * D:4 * D],
                        A[:, BUO:BUO + D], wu_s]

            wait("gpsimd", "s_x", 64)
            for g in range(NH):
                wait("gpsimd", "s_wfog", g - 3)
                dma("gpsimd", wof_bufs[g % 4], wout_h[g * P:(g + 1) * P, :],
                    "s_stg2")
            for g in range(NH):
                wait("sync", "s_wvr", g - 2)
                dma("sync", wvf_bufs[g % 3], wqv_h[g * P:(g + 1) * P, :],
                    "s_stg")
                wait("vector", "s_stg")
                def fold(E, g=g):
                    bb = wvf_bufs[g % 3]
                    sap = bass.AP(bb.tensor, bb.offset,
                                  [list(bb.ap[0]), [NH, P], [1, NH]])
                    return E.tensor_reduce(
                        wvf[:, g * P:(g + 1) * P], sap, AX.X, ALU.add)
                op("vector", "s_wvr", fold)
                wait("vector", "s_stg2", 16 * (g + 1))
                fence("vector")
                def wacc(E, g=g):
                    a = wof_bufs[g % 4]
                    dst = wof if g % 2 == 0 else wacc2
                    if g <= 1:
                        return E.tensor_copy(dst, a)
                    return E.tensor_add(dst, dst, a)
                op("vector", "s_wfog", wacc)
            fence("vector")
            op("vector", "s_wvv", lambda E: E.tensor_add(wof, wof, wacc2))

            wait("sync", "s_wvr")
            dma("sync", wd_s.rearrange("p (k c) -> p k c", c=P),
                wd_h.rearrange("(k p) c -> p k c", p=P), "s_wm")
            wait("sync", "s_wfog")
            dma("sync", wu_s, wu_h, "s_wm")
            dma("sync", bu_s, bu_h, "s_wm")

            wait("sync", "s_wvr")
            wait("sync", "s_wfog")
            for tt in range(TTH):
                dma("sync", xt(1, tt), x_h[512 + tt * P:512 + (tt + 1) * P, :],
                    "s_x")

            for h in range(HALVES):
                tok0 = h * 512

                if h == 1:
                    ln1_pipe(1)

                if stage == "ln1" and h == 0:
                    dump_stop(fm, 8192, ["s_fme"])
                    return

                wait("tensor", "s_bvf")
                wait("tensor", "s_fme")
                wait("tensor", "s_evq", h)
                wait("tensor", "s_evo", 16 * h)
                wait("tensor", "s_lg", 4 * h)
                for k in range(KC):
                    wait("tensor", "s_wvr", k + 1)
                    op("tensor", "s_mmq",
                       lambda E, k=k: E.matmul(
                           ps_sm[:, 0:512], wvf[:, k * P:(k + 1) * P],
                           fm[:, k * 512:(k + 1) * 512],
                           start=(k == 0), stop=False))
                op("tensor", "s_mmq",
                   lambda E: E.matmul(ps_sm[:, 0:512],
                                      bvf_row, ones_s[:, 0:512],
                                      start=False, stop=True))
                wait("scalar", "s_mmq")
                op("scalar", "s_evq", lambda E: E.copy(vsf, ps_sm[:, 0:512]))

                if stage == "vsum" and h == 0:
                    dump_stop(A[:, WVF:WVF + 12800], 12800,
                              ["s_evq", "s_wvr", "s_wv", "s_bvf", "s_bo"])
                    return

                ybank = [ps_mm[0], ps_mm[1], ps_sm, ps_we]
                for tt in range(TTH):
                    for nb in range(4):
                        oidx = (h * TTH + tt) * 4 + nb
                        wait("tensor", "s_wvv")
                        wait("tensor", "s_evq")
                        wait("tensor", "s_lg", 4 * h)
                        wait("tensor", "s_hw", h)
                        wait("tensor", "s_evo", oidx - 3)
                        op("tensor", "s_mmo",
                           lambda E, tt=tt, nb=nb, oidx=oidx: E.matmul(
                               ybank[oidx % 4][:, 0:512],
                               vsf[:, tt * P:(tt + 1) * P],
                               wof[:, nb * 512:(nb + 1) * 512],
                               start=True, stop=True))
                        wait("vector", "s_mmo")
                        wait("vector", "s_x")
                        wait("vector", "s_bo", 4)
                        op("vector", "s_evo",
                           lambda E, h=h, tt=tt, nb=nb, oidx=oidx:
                           E.scalar_tensor_tensor(
                               out=xt(h, tt)[:, nb * 512:(nb + 1) * 512],
                               in0=ybank[oidx % 4][:, 0:512], scalar=1.0,
                               in1=xt(h, tt)[:, nb * 512:(nb + 1) * 512],
                               op0=ALU.mult, op1=ALU.add))
                    fence("vector")
                    op("vector", "s_bta",
                       lambda E, h=h, tt=tt: E.tensor_add(xt(h, tt), xt(h, tt),
                                                          botT))

                if stage == "wout" and h == 0:
                    dump_stop(A[:, XA:XA + 8192], 8192, ["s_evo", "s_bta"])
                    return

                wait("scalar", "s_var")
                wait("vector", "s_var")
                for tt in range(TTH):
                    wait("vector", "s_bta", h * TTH + tt + 1)
                    wait("scalar", "s_bta", h * TTH + tt + 1)
                    op("scalar", "s_sq",
                       lambda E, h=h, tt=tt: E.activation(
                           jnk, xt(h, tt), ACT.Square,
                           accum_out=st_sq[:, tt:tt + 1]))
                    run("vector", lambda E, h=h, tt=tt: E.reduce_sum(
                        st_sum[:, tt:tt + 1], xt(h, tt), AX.X))
                wait("vector", "s_sq")
                fence("vector")
                run("vector", lambda E: E.tensor_scalar_mul(st_mean, st_sum, 1.0 / D))
                run("vector", lambda E: E.tensor_scalar_mul(st_ms, st_sq, 1.0 / D))
                fence("vector")
                run("vector", lambda E: E.tensor_mul(st_var, st_mean, st_mean))
                fence("vector")
                op("vector", "s_var",
                   lambda E: E.tensor_sub(st_var, st_ms, st_var))
                wait("scalar", "s_var")
                op("scalar", "s_std",
                   lambda E: E.activation(st_sqv, st_var, ACT.Sqrt,
                                          bias=np.float32(EPS)))
                wait("vector", "s_std")
                run("vector", lambda E: E.reciprocal(st_rstd, st_sqv))
                fence("vector")
                for tt in range(TTH):
                    wait("vector", "s_tpx", cnt["s_tpx"] - 4)
                    run("vector", lambda E, h=h, tt=tt: E.tensor_scalar_sub(
                        scr(tt), xt(h, tt), st_mean[:, tt:tt + 1]))
                    fence("vector")
                    op("vector", "s_xn",
                       lambda E, tt=tt: E.tensor_scalar_mul(
                           scr(tt), scr(tt), st_rstd[:, tt:tt + 1]))
                    wait("tensor", "s_xn")
                    wait("tensor", "s_fme")
                    for b in range(4):
                        def tr4b(E, tt=tt, b=b):
                            ins = None
                            for q in range(4):
                                k = b * 4 + q
                                ins = E.transpose(ps_tp[b][:, q * P:(q + 1) * P],
                                                  scr(tt)[:, k * P:(k + 1) * P],
                                                  ident_s)
                            return ins
                        op("tensor", "s_tpx", tr4b)
                    wait("scalar", "s_tpx")
                    for b in range(4):
                        def ev4b(E, tt=tt, b=b):
                            dst = fm[:, b * 4 * 512 + tt * P:]
                            dst = bass.AP(dst.tensor, dst.offset,
                                          [list(dst.ap[0]), [512, 4], [1, P]])
                            return E.copy(dst, ps_tp[b][:].rearrange(
                                "p (q t) -> p q t", q=4))
                        op("scalar", "s_fme", ev4b)

                for tt in range(TTH):
                    wait("tensor", "s_fme")
                    wait("tensor", "s_evo")
                    wait("tensor", "s_evq")
                    wait("tensor", "s_lg", 4 * h + tt)
                    def mmr(E, tt=tt):
                        ins = None
                        for k in range(KC):
                            ins = E.matmul(ps_sm[:, 0:8], fm_st(tt, k),
                                           wr_s[:, k * 8:(k + 1) * 8],
                                           start=(k == 0), stop=False)
                        ins = E.matmul(ps_sm[:, 0:8], ones_s[:, 0:P],
                                       bias_row[:, 4096:4104],
                                       start=False, stop=True)
                        return ins
                    op("tensor", "s_rps", mmr)
                    wait("scalar", "s_rps")
                    op("scalar", "s_lg",
                       lambda E, tt=tt: E.activation(gate_tile(tt, 0),
                                                     ps_sm[:, 0:8], ACT.Exp))

                wait("vector", "s_lg")
                er3 = g_er.rearrange("p (t e) -> p t e", e=8)
                msk3 = g_msk.rearrange("p (t e) -> p t e", e=8)
                wg3 = g_wg.rearrange("p (t e) -> p t e", e=8)
                def b1(st):
                    return st.unsqueeze(2).broadcast_to([P, 4, 8])
                run("vector", lambda E: E.tensor_reduce(gm1, er3, AX.X, ALU.max))
                fence("vector")
                run("vector", lambda E: E.tensor_tensor(msk3, er3, b1(gm1),
                                                        ALU.is_lt))
                fence("vector")
                run("vector", lambda E: E.tensor_mul(g_msk, g_msk, g_er))
                fence("vector")
                run("vector", lambda E: E.tensor_reduce(gm2, msk3, AX.X, ALU.max))
                fence("vector")
                run("vector", lambda E: E.tensor_tensor(msk3, er3, b1(gm2),
                                                        ALU.is_ge))
                run("vector", lambda E: E.tensor_add(gdn, gm1, gm2))
                fence("vector")
                run("vector", lambda E: E.tensor_mul(g_wg, g_msk, g_er))
                run("vector", lambda E: E.reciprocal(grd, gdn))
                fence("vector")
                op("vector", "s_gt",
                   lambda E: E.tensor_tensor(wg3, wg3, b1(grd), ALU.mult))

                wait("tensor", "s_wm")
                wait("tensor", "s_fme")
                wait("tensor", "s_hfm", h)
                def mmd(E):
                    ins = None
                    for k in range(KC):
                        ins = E.matmul(ps_tp[1][:, 0:512],
                                       wd_s[:, k * P:(k + 1) * P],
                                       fm[:, k * 512:(k + 1) * 512],
                                       start=(k == 0), stop=(k == KC - 1))
                    return ins
                op("tensor", "s_dps", mmd)
                wait("scalar", "s_dps")
                op("scalar", "s_hfm",
                   lambda E: E.activation(h_fm, ps_tp[1][:, 0:512], ACT.Relu,
                                          bias=bd_s))

                wait("tensor", "s_gt")
                wait("tensor", "s_wfm", h)
                def trw(E):
                    ins = None
                    for tt in range(TTH):
                        ins = E.transpose(ps_tp[0][0:8, tt * P:(tt + 1) * P],
                                          gate_tile(tt, 2), ident_s)
                    return ins
                op("tensor", "s_tpw", trw)
                wait("scalar", "s_tpw")
                op("scalar", "s_wfm",
                   lambda E: E.copy(w_fm, ps_tp[0][0:8, 0:512]))

                wait("tensor", "s_wfm")
                wait("tensor", "s_hw", h)
                wait("tensor", "s_evo")
                op("tensor", "s_eps",
                   lambda E: E.matmul(ps_we[:, 0:512], emat_s, w_fm,
                                      start=True, stop=True))
                wait("vector", "s_eps")
                wait("vector", "s_hfm")
                op("vector", "s_hw",
                   lambda E: E.tensor_tensor(hw_fm, h_fm, ps_we[:, 0:512],
                                             ALU.mult))

                if h == 0:
                    ln1_stats(1)

                fence("vector")
                for tt in range(TTH):
                    for nt in range(4):
                        uidx = (h * TTH + tt) * 4 + nt
                        wait("tensor", "s_hw")
                        wait("tensor", "s_evo")
                        wait("tensor", "s_x2", uidx - 1)
                        def mmu(E, tt=tt, nt=nt, uidx=uidx):
                            pb = ps_mm[uidx % 2]
                            E.matmul(pb[:, 0:512],
                                     hw_fm[:, tt * P:(tt + 1) * P],
                                     wu_s[:, nt * 512:(nt + 1) * 512],
                                     start=True, stop=False)
                            return E.matmul(pb[:, 0:512],
                                            w_fm[:, tt * P:(tt + 1) * P],
                                            bu_s[:, nt * 512:(nt + 1) * 512],
                                            start=False, stop=True)
                        op("tensor", "s_ups", mmu)
                        wait("vector", "s_ups")
                        op("vector", "s_x2",
                           lambda E, h=h, tt=tt, nt=nt, uidx=uidx:
                           E.scalar_tensor_tensor(
                               out=xt(h, tt)[:, nt * 512:(nt + 1) * 512],
                               in0=ps_mm[uidx % 2][:, 0:512], scalar=1.0,
                               in1=xt(h, tt)[:, nt * 512:(nt + 1) * 512],
                               op0=ALU.mult, op1=ALU.add))
                    wait("gpsimd", "s_x2")
                    dma("gpsimd", y_h[tok0 + tt * P: tok0 + (tt + 1) * P, :],
                        xt(h, tt), "s_out")

            wait("gpsimd", "s_out")

        @blk.sync
        def _(E):
            program(E, "sync")

        @blk.tensor
        def _(E):
            program(E, "tensor")

        @blk.scalar
        def _(E):
            program(E, "scalar")

        @blk.vector
        def _(E):
            program(E, "vector")

        @blk.gpsimd
        def _(E):
            program(E, "gpsimd")

        sem_ctx.close()
    return nc


def kernel(L0_attn_x, L0_attn_Wqkv, L0_attn_bqkv, L0_attn_Wout, L0_attn_bout,
           L0_ffn_Wr, L0_ffn_br, L0_ffn_Wd, L0_ffn_bd, L0_ffn_Wu, L0_ffn_bu):
    x = np.ascontiguousarray(np.asarray(L0_attn_x, dtype=np.float32).reshape(8192, D))
    wqkv_v = np.ascontiguousarray(
        np.asarray(L0_attn_Wqkv, np.float32).reshape(D, NH, 384)[:, :, 256:384]
        .transpose(0, 2, 1)).reshape(D, D)
    bqkv_v = np.ascontiguousarray(
        np.asarray(L0_attn_bqkv, np.float32).reshape(NH, 384)[:, 256:384]
        .T).reshape(1, D)
    wout = np.ascontiguousarray(np.asarray(L0_attn_Wout, np.float32).reshape(D, D))
    bout = np.ascontiguousarray(np.asarray(L0_attn_bout, np.float32).reshape(1, D))
    wr = np.ascontiguousarray(np.asarray(L0_ffn_Wr, np.float32).reshape(D, 8))
    br = np.ascontiguousarray(np.asarray(L0_ffn_br, np.float32).reshape(1, 8))
    wd = np.ascontiguousarray(
        np.asarray(L0_ffn_Wd, np.float32).transpose(1, 0, 2).reshape(D, P))
    bd = np.ascontiguousarray(np.asarray(L0_ffn_bd, np.float32).reshape(P, 1))
    wu = np.ascontiguousarray(np.asarray(L0_ffn_Wu, np.float32).reshape(P, D))
    bu = np.ascontiguousarray(np.asarray(L0_ffn_bu, np.float32).reshape(8, D))
    ident = np.eye(P, dtype=np.float32)
    onesrow = np.ones((1, 512), dtype=np.float32)
    emat = np.zeros((8, P), dtype=np.float32)
    for e in range(8):
        emat[e, e * 16:(e + 1) * 16] = 1.0

    nc = build_nc()
    in_maps = []
    for c in range(8):
        in_maps.append({
            "x": x[c * TOK_CORE:(c + 1) * TOK_CORE],
            "wqkv_v": wqkv_v, "bqkv_v": bqkv_v, "wout": wout, "bout": bout,
            "wr": wr, "br": br, "wd": wd, "bd": bd, "wu": wu, "bu": bu,
            "ident": ident, "onesrow": onesrow, "emat": emat,
        })
    res = run_bass_kernel_spmd(nc, in_maps, core_ids=list(range(8)))
    out = np.concatenate([np.asarray(r["y"]) for r in res.results], axis=0)
    return out.reshape(1, 8192, D)
